# version 04 (final)
# baseline (speedup 1.0000x reference)
import os
import numpy as np
import ml_dtypes
from contextlib import ExitStack

import concourse.bass as bass
import concourse.mybir as mybir
from concourse.bass_utils import run_bass_kernel_spmd

F32 = mybir.dt.float32
BF16 = mybir.dt.bfloat16
AF = mybir.ActivationFunctionType
ALU = mybir.AluOpType
AX = mybir.AxisListType.X

B_PER_CORE = 2
SEQ = 4096
C = 1024
H = 16
K = 64
CH = 128
TT = 512
GROUPS = TT // CH
NTILE = B_PER_CORE * SEQ // TT
NCHUNK = 8
EPS = 1e-5
GN_EPS = 64.0 * EPS

W_NAMES = ["ar", "ak", "ag", "av", "ao", "fr", "fk", "fv"]
_DBG = []
CONST_F32S = [EPS, GN_EPS, -1.0, 1.0 / 32.0, 0.125, 128.0, 1.0, 1.0 / 64.0]
_PHASE = int(os.environ.get("ATH_PHASE", "99"))
_TAP = int(os.environ.get("ATH_TAP", "0"))
_P5 = int(os.environ.get("ATH_P5", "6"))
_ATTSER = int(os.environ.get("ATH_ATTSER", "0"))
_NT = int(os.environ.get("ATH_NTILES", "16"))



class _Op:
    __slots__ = ("eng", "fn", "deps", "fdeps", "kind", "chan", "marked", "cnt",
                 "hz", "small")

    def __init__(self, eng, fn, deps, kind, chan=None, hz=False, small=False):
        self.eng = eng
        self.fn = fn
        self.deps = deps
        self.fdeps = []
        self.kind = kind
        self.chan = chan
        self.marked = False
        self.cnt = 0
        self.hz = hz
        self.small = small


class _Buf:
    __slots__ = ("w", "r")

    def __init__(self):
        self.w = None
        self.r = []


class _Sched:
    ENGINES = ("sync", "gpsimd", "vector", "scalar", "tensor")

    def __init__(self):
        self.ops = {e: [] for e in self.ENGINES}
        self.filler_fn = None

    def add(self, eng, fn, reads=(), writes=(), kind="x", extra=(), chan=None,
            hz=False, small=False):
        deps = []
        for b in reads:
            if b.w is not None:
                deps.append(b.w)
        for b in writes:
            if b.w is not None:
                deps.append(b.w)
            deps.extend(b.r)
        deps.extend(extra)
        seen = set()
        ud = []
        for d in deps:
            if id(d) not in seen:
                seen.add(id(d))
                ud.append(d)
        if kind == "d":
            assert chan is not None
        op = _Op(eng, fn, ud, kind, chan, hz, small)
        if eng == "vector":
            recent = self.ops["vector"][-3:]
            for d in ud:
                if d.eng == "vector" and (d.hz or (d.small and d in recent)):
                    op.fdeps.append(d)
        self.ops[eng].append(op)
        for b in reads:
            b.r.append(op)
        for b in writes:
            b.w = op
            b.r = []
        return op

    def finalize(self):
        for eng in self.ENGINES:
            for op in self.ops[eng]:
                if op.kind == "d":
                    op.marked = True
                for d in op.deps:
                    if d.eng != op.eng or d.kind == "d":
                        d.marked = True
                for d in op.fdeps:
                    d.marked = True
        ccnt = {}
        for eng in self.ENGINES:
            cx = 0
            for op in self.ops[eng]:
                if op.kind == "d":
                    ccnt[op.chan] = ccnt.get(op.chan, 0) + 1
                    op.cnt = ccnt[op.chan]
                elif op.marked:
                    cx += 1
                    op.cnt = cx
        self.chan_totals = ccnt

    def emit(self, block, nc, sems):
        for eng_name in self.ENGINES:
            ops = self.ops[eng_name]
            if not ops:
                continue

            def make(eng_name, ops):
                def body(eng):
                    waited = {}
                    for op in ops:
                        for d in op.deps + op.fdeps:
                            if d in op.fdeps:
                                pass
                            elif d.eng == op.eng and d.kind != "d":
                                continue
                            key = ("c", d.chan) if d.kind == "d" else (d.eng, "x")
                            if (d.kind == "d" and d.chan == "su"
                                    and not (op.kind == "d" and op.chan == "su")):
                                val = 16 * self.chan_totals[d.chan]
                            else:
                                val = d.cnt * (16 if d.kind == "d" else 1)
                            if waited.get(key, 0) >= val:
                                continue
                            eng.wait_ge(sems[key], val)
                            waited[key] = val
                        inst = op.fn(eng)
                        if op.marked and inst is not None:
                            key = ("c", op.chan) if op.kind == "d" else (op.eng, "x")
                            inst.then_inc(sems[key], 16 if op.kind == "d" else 1)
                return body

            getattr(block, eng_name)(make(eng_name, ops))



def _build_program():
    nc = bass.Bass("TRN2", target_bir_lowering=False,
                   detect_race_conditions=False)

    x_d = nc.dram_tensor("x", [B_PER_CORE, SEQ, C], F32, kind="ExternalInput").ap()
    out_d = nc.dram_tensor("out", [B_PER_CORE, SEQ, C], F32, kind="ExternalOutput").ap()
    wt_d = {n: nc.dram_tensor("wT_" + n, [C, C], F32, kind="ExternalInput").ap()
            for n in W_NAMES}
    vec_d = {n: nc.dram_tensor("v_" + n, [C], F32, kind="ExternalInput").ap()
             for n in ["mixr", "mixk", "mixv", "mixg", "mur", "muk", "lnw", "lnb"]}
    td_d = nc.dram_tensor("td", [16], F32, kind="ExternalInput").ap()
    u_d = nc.dram_tensor("u", [16], F32, kind="ExternalInput").ap()
    eyebf_d = nc.dram_tensor("c_eyebf", [128, 128], BF16, kind="ExternalInput").ap()
    ind16bf_d = nc.dram_tensor("c_ind16bf", [16, 1024], BF16, kind="ExternalInput").ap()
    indpcbf_d = nc.dram_tensor("c_indpcbf", [16, 1024], BF16, kind="ExternalInput").ap()
    diffm1_d = nc.dram_tensor("c_diffm1", [128, 128], F32, kind="ExternalInput").ap()
    tri_d = nc.dram_tensor("c_tri", [128, 128], F32, kind="ExternalInput").ap()
    eyef_d = nc.dram_tensor("c_eyef", [128, 128], F32, kind="ExternalInput").ap()
    iota16_d = nc.dram_tensor("c_iota16", [16, 128], F32, kind="ExternalInput").ap()
    riota16_d = nc.dram_tensor("c_riota16", [16, 128], F32, kind="ExternalInput").ap()
    f32s_d = nc.dram_tensor("c_f32s", [8], F32, kind="ExternalInput").ap()
    tap_d = {}
    if _TAP:
        for nm, shp, dt in [("yfm", [128, 4104], BF16), ("r", [128, 4096], BF16),
                            ("k", [128, 4096], BF16), ("g", [128, 4096], BF16),
                            ("xsr", [128, 4096], BF16), ("xsk", [128, 4096], BF16),
                            ("xsv", [128, 4096], BF16), ("xsg", [128, 4096], BF16),
                            ("vtm", [128, 1024], BF16), ("kkw", [128, 1024], BF16),
                            ("rrw", [128, 1024], BF16), ("attwe", [128, 512], BF16),
                            ("attwo", [128, 512], BF16), ("xo", [128, 1024], BF16),
                            ("ttm", [128, 1024], BF16), ("yg", [128, 4096], BF16),
                            ("xmid", [128, 4096], F32), ("sbf", [128, 1024], BF16),
                            ("wm", [128, 2048], BF16), ("wk", [128, 1024], BF16),
                            ("wb", [128, 1024], BF16), ("ws", [128, 8], F32),
                            ("y2fm", [128, 4104], BF16), ("kact", [128, 4096], BF16),
                            ("r2", [128, 1024], BF16), ("fo", [128, 1024], BF16),
                            ("gs", [128, 80], F32), ("sqs", [128, 1024], BF16),
                            ("x", [128, 4096], F32), ("cen", [128, 1024], BF16)]:
            tap_d[nm] = nc.dram_tensor("tap_" + nm, shp, dt, kind="ExternalOutput").ap()
    wsc_d = {n: nc.dram_tensor("wsc_" + n, [C, C], BF16).ap() for n in W_NAMES}
    sml_d = nc.dram_tensor("smlscr", [64], F32).ap()

    S = _Sched()

    with ExitStack() as ctx:
        def sb(name, shp, dt):
            return ctx.enter_context(nc.sbuf_tensor(name, shp, dt))

        def ps(name, shp, dt):
            return ctx.enter_context(nc.psum_tensor(name, shp, dt))

        xb = [sb("xb%d" % i, [128, 4 * C], F32) for i in range(2)]
        y_fm = sb("y_fm", [128, 8 * 513], BF16)
        xs_fm = {n: sb("xs_" + n, [128, 4096], BF16)
                 for n in ["r", "k", "v", "g"]}
        r_fm = sb("r_fm", [128, 4096], BF16)
        k_fm = sb("k_fm", [128, 4096], BF16)
        g_fm = sb("g_fm", [128, 4096], BF16)
        yg_fm = sb("yg_fm", [128, 4096], BF16)
        wbuf = [sb("wb%d" % i, [128, 8192], BF16) for i in range(2)]
        v_tm = [sb("v_tm%d" % i, [128, 1024], BF16) for i in range(2)]
        kkw = [sb("kkw%d" % i, [128, 1024], BF16) for i in range(2)]
        rrw = [sb("rrw%d" % i, [128, 1024], BF16) for i in range(2)]
        attw = [sb("attw%d" % i, [128, 512], BF16) for i in range(2)]
        xo_sb = [sb("xo%d" % i, [128, 1024], BF16) for i in range(2)]
        sqs = [sb("sqs%d" % i, [128, 1024], BF16) for i in range(2)]
        t_tm = [sb("ttm%d" % i, [128, 1024], BF16) for i in range(2)]
        y_tm = [sb("ytm%d" % i, [128, 1024], BF16) for i in range(2)]
        r2_tm = [sb("r2tm%d" % i, [128, 1024], BF16) for i in range(2)]
        fo_bf = [sb("fo%d" % i, [128, 1024], BF16) for i in range(2)]
        t_bf = [sb("tbf%d" % i, [128, 1024], BF16) for i in range(2)]
        d_sc = sb("d_sc", [128, 512], BF16)
        junk_s = sb("junk_s", [128, 1024], BF16)
        s_f32 = [sb("sf%d" % b, [128, 1024], F32) for b in range(2)]
        s_bf = [sb("sb%d" % b, [128, 1024], BF16) for b in range(2)]
        ycar = [sb("ycar%d" % b, [128, 8], BF16) for b in range(2)]
        y2car = [sb("y2car%d" % b, [128, 8], BF16) for b in range(2)]
        stat = [sb("stat%d" % i, [128, 8], F32) for i in range(8)]
        gstat = [sb("gstat%d" % i, [128, 80], F32) for i in range(2)]
        wk_tab = sb("wk_tab", [128, 1024], BF16)
        wb_tab = sb("wb_tab", [128, 1024], BF16)
        wm_tab = sb("wm_tab", [128, 2048], BF16)
        wspair = sb("wspair", [128, 8], F32)
        mixv_sb = {n: sb("mx_" + n, [128, 8], F32)
                   for n in ["mixr", "mixk", "mixv", "mixg", "mur", "muk", "lnw", "lnb"]}
        ub_sb = sb("ub", [128, 16], F32)
        ne1b_sb = sb("ne1b", [128, 16], F32)
        wsb_sb = sb("wsb", [128, 16], F32)
        eyebf = sb("eyebf", [128, 128], BF16)
        cst = sb("cst", [128, 8], F32)
        for i, v in enumerate(CONST_F32S):
            nc.const_aps.register(F32, float(v), cst[:, i:i + 1])
        td_p = sb("td_p", [16, 1], F32)
        u_p = sb("u_p", [16, 1], F32)
        e1_p = sb("e1_p", [16, 1], F32)
        ne1_p = sb("ne1_p", [16, 1], F32)
        ws_p = sb("ws_p", [16, 1], F32)
        diffm1 = xb[0][:, 0:128]
        tri_f = xb[0][:, 128:256]
        eyef = xb[0][:, 256:384]
        texp = xb[0][:, 384:512]
        texp2 = xb[0][:, 512:640]
        iota16 = xb[0][0:16, 640:768]
        riota16 = xb[0][0:16, 768:896]
        wkh = xb[0][0:16, 1024:1152]
        wbh = xb[0][0:16, 1152:1280]
        ind16bf = wbuf[0][0:16, 0:1024]
        indpcbf = wbuf[0][0:16, 1024:2048]
        wkh_bf = wbuf[0][0:16, 2048:2176]
        wbh_bf = wbuf[0][0:16, 2176:2304]

        pp = [ps("pp%d" % i, [128, 512], F32) for i in range(2)]
        pact = ps("pact", [128, 1024], F32)
        ptr = ps("ptr", [128, 1024], BF16)
        patt = ps("patt", [128, 512], F32)
        pout = ps("pout", [128, 1024], F32)

        B = _Buf
        b_xb = [[B() for _ in range(4)] for _ in range(2)]
        b_yfm = B()
        b_xs = {n: B() for n in xs_fm}
        b_r, b_k, b_g, b_yg = B(), B(), B(), B()
        b_w = [B() for _ in range(2)]
        b_vtm = [B(), B()]
        b_kkw = [B(), B()]
        b_rrw = [B(), B()]
        b_attw = [B(), B()]
        b_xo = [B(), B()]
        b_sqs = [B(), B()]
        b_ttm = [B(), B()]
        b_ytm = [B(), B()]
        b_r2 = [B(), B()]
        b_fo = [B(), B()]
        b_tbf = [B(), B()]
        b_dsc = B()
        b_junk = B()
        b_sf = [B(), B()]
        b_sbf = [B(), B()]
        b_ycar = [B(), B()]
        b_y2car = [B(), B()]
        b_stat = [B() for _ in range(8)]
        b_gstat = [B(), B()]
        b_tabs = B()
        b_pp = [B(), B()]
        b_pact = B()
        b_ptr = B()
        b_patt = [B() for _ in range(4)]
        b_pout = [B() for _ in range(16)]
        b_pout_all = B()

        fillb = sb("fillb", [128, 32], F32)
        S.filler_fn = lambda e: e.tensor_copy(fillb[:, 0:16], fillb[:, 16:32])
        b_wsc = _Buf()
        sems = {}

        op_wcast = {}
        for n in W_NAMES:
            op_wcast[n] = S.add("gpsimd", (lambda n: lambda e: e.dma_start(
                wsc_d[n][:, :], wt_d[n][:, :]))(n), kind="d", chan="wc",
                writes=[b_wsc])

        def dma(eng, dst, src, reads=(), writes=(), extra=(), slow=False, chan="su"):
            return S.add(eng, lambda e: e.dma_start(dst, src,
                                                    allow_slow_non_contiguous=slow),
                         reads=reads, writes=writes, kind="d", extra=extra, chan=chan)

        for n, t in list(vec_d.items()):
            dma("sync", mixv_sb[n][:], bass.AP(t.tensor, 0, [[1, 128], [128, 8]]),
                writes=[b_tabs], slow=True)
        dma("sync", td_p[:], bass.AP(td_d.tensor, 0, [[1, 16], [1, 1]]),
            writes=[b_tabs], slow=True)
        dma("sync", u_p[:], bass.AP(u_d.tensor, 0, [[1, 16], [1, 1]]),
            writes=[b_tabs], slow=True)
        dma("sync", ub_sb[:], bass.AP(u_d.tensor, 0, [[0, 128], [1, 16]]),
            writes=[b_tabs])
        dma("sync", eyebf[:], eyebf_d[:], writes=[b_tabs])
        dma("sync", cst[:], bass.AP(f32s_d.tensor, 0, [[0, 128], [1, 8]]),
            writes=[b_tabs])
        b_scr_x = b_xb[0][0]
        b_scr_x2 = b_xb[0][1]
        b_scr_w = b_w[0]
        for dst, src, bw in [(diffm1, diffm1_d[:], b_scr_x), (tri_f, tri_d[:], b_scr_x),
                             (eyef, eyef_d[:], b_scr_x), (iota16, iota16_d[:], b_scr_x),
                             (riota16, riota16_d[:], b_scr_x),
                             (ind16bf, ind16bf_d[:], b_scr_w),
                             (indpcbf, indpcbf_d[:], b_scr_w)]:
            dma("sync", dst, src, writes=[bw])

        for b in range(2):
            S.add("vector", (lambda b: lambda e: e.memset(s_f32[b][:], 0.0))(b),
                  writes=[b_sf[b]])
            S.add("vector", (lambda b: lambda e: e.memset(s_bf[b][:], 0.0))(b),
                  writes=[b_sbf[b]])
            S.add("vector", (lambda b: lambda e: e.memset(ycar[b][:], 0.0))(b),
                  writes=[b_ycar[b]])
            S.add("vector", (lambda b: lambda e: e.memset(y2car[b][:], 0.0))(b),
                  writes=[b_y2car[b]])

        S.add("scalar", lambda e: e.activation(e1_p[:], td_p[:], AF.Exp),
              reads=[b_tabs], writes=[b_tabs])
        S.add("vector", lambda e: e.tensor_scalar(ne1_p[:], e1_p[:], -1.0, None, ALU.mult),
              reads=[b_tabs], writes=[b_tabs])
        S.add("scalar", lambda e: e.activation(wkh, riota16, AF.Exp, scale=ne1_p[:]),
              reads=[b_tabs, b_scr_x], writes=[b_scr_x2])
        S.add("scalar", lambda e: e.activation(wbh, iota16, AF.Exp, scale=ne1_p[:]),
              reads=[b_tabs, b_scr_x], writes=[b_scr_x2])
        S.add("scalar", lambda e: e.activation(ws_p[:], ne1_p[:], AF.Exp, scale=128.0),
              reads=[b_tabs], writes=[b_tabs])
        S.add("vector", lambda e: e.tensor_copy(wkh_bf, wkh),
              reads=[b_scr_x2], writes=[b_scr_w])
        S.add("vector", lambda e: e.tensor_copy(wbh_bf, wbh),
              reads=[b_scr_x2], writes=[b_scr_w])
        b_sml = _Buf()
        dma("gpsimd", bass.AP(sml_d.tensor, 0, [[1, 16], [1, 1]]), ws_p[:],
            reads=[b_tabs], writes=[b_sml], slow=True, chan="sml")
        dma("gpsimd", wsb_sb[:], bass.AP(sml_d.tensor, 0, [[0, 128], [1, 16]]),
            reads=[b_sml], writes=[b_tabs], chan="sml")
        dma("gpsimd", bass.AP(sml_d.tensor, 16, [[1, 16], [1, 1]]), ne1_p[:],
            reads=[b_tabs], writes=[b_sml], slow=True, chan="sml")
        dma("gpsimd", ne1b_sb[:], bass.AP(sml_d.tensor, 16, [[0, 128], [1, 16]]),
            reads=[b_sml], writes=[b_tabs], chan="sml")
        for q in range(8):
            S.add("scalar", (lambda q: lambda e: e.copy(
                wspair[0:64, q:q + 1], wsb_sb[0:64, 2 * q:2 * q + 1]))(q),
                reads=[b_tabs], writes=[b_tabs])
            S.add("scalar", (lambda q: lambda e: e.copy(
                wspair[64:128, q:q + 1], wsb_sb[64:128, 2 * q + 1:2 * q + 2]))(q),
                reads=[b_tabs], writes=[b_tabs])
        for half in range(2):
            S.add("tensor", (lambda half: lambda e: e.matmul(
                pact[:, 512 * half:512 * half + 512], wkh_bf,
                ind16bf[:, 512 * half:512 * half + 512], start=True, stop=True))(half),
                reads=[b_scr_w], writes=[b_pact])
        S.add("scalar", lambda e: e.copy(wk_tab[:], pact[:]),
              reads=[b_pact], writes=[b_tabs])
        for c_ in range(8):
            S.add("tensor", (lambda c_: lambda e: e.matmul(
                pact[:, 0:128],
                indpcbf[:, 128 * c_:128 * c_ + 128], wbh_bf,
                start=True, stop=True))(c_),
                reads=[b_scr_w], writes=[b_pact])
            S.add("scalar", (lambda c_: lambda e: e.copy(
                wb_tab[:, 128 * c_:128 * c_ + 128], pact[:, 0:128]))(c_),
                reads=[b_pact], writes=[b_tabs])
        for h in range(16):
            S.add("scalar", (lambda h: lambda e: e.activation(
                texp, diffm1, AF.Exp, scale=ne1b_sb[:, h:h + 1]))(h),
                reads=[b_tabs, b_scr_x], writes=[b_scr_x])
            S.add("vector", lambda e: e.tensor_mul(texp2, texp, tri_f),
                  reads=[b_scr_x], writes=[b_scr_x])
            S.add("vector", (lambda h: lambda e: e.scalar_tensor_tensor(
                wm_tab[:, 128 * h:128 * h + 128], eyef, ub_sb[:, h:h + 1],
                texp2, ALU.mult, ALU.add))(h),
                reads=[b_tabs, b_scr_x], writes=[b_tabs])

        stat_i = [0]

        def new_stat():
            i = stat_i[0] % 8
            stat_i[0] += 1
            return stat[i], b_stat[i]

        wload_i = [0]

        def load_weight(n):
            slot = wload_i[0] % 2
            wload_i[0] += 1
            src = bass.AP(wsc_d[n].tensor, 0, [[1024, 128], [128 * 1024, 8], [1, 1024]])
            dma("sync", wbuf[slot][:], src, reads=[b_wsc], writes=[b_w[slot]],
                chan="w%d" % slot)
            return slot

        def ln_stats(xap, b_x, scale_sq, n_inv, eps, square_from=None):
            st, b_st = new_stat()
            sm = st[:, 0:1]; m1 = st[:, 1:2]; m2 = st[:, 2:3]
            vn = st[:, 3:4]; sg = st[:, 4:5]; ri = st[:, 5:6]; mb = st[:, 6:7]
            S.add("vector", lambda e: e.reduce_sum(sm, xap, AX),
                  reads=[b_x], writes=[b_st], hz=True)
            S.add("scalar", lambda e: e.activation(junk_s[:], xap, AF.Square,
                                                   scale=scale_sq, accum_out=m2),
                  reads=[b_x, b_tabs], writes=[b_junk, b_st])
            S.add("vector", lambda e: e.tensor_scalar(m1, sm, n_inv, None, ALU.mult),
                  reads=[b_st], writes=[b_st], small=True)
            S.add("vector", lambda e: e.scalar_tensor_tensor(vn, m1, m1, m2,
                                                             ALU.mult, ALU.subtract),
                  reads=[b_st], writes=[b_st], small=True)
            S.add("scalar", lambda e: e.activation(sg, vn, AF.Sqrt, scale=-1.0, bias=eps),
                  reads=[b_st, b_tabs], writes=[b_st])
            S.add("vector", lambda e: e.reciprocal(ri, sg),
                  reads=[b_st], writes=[b_st], hz=True)
            return m1, ri, mb, b_st

        b_tapchain = _Buf()
        def tap(nm, src_ap, bufs):
            if _TAP and nm in tap_d:
                dma("gpsimd", tap_d[nm][:], src_ap, reads=list(bufs),
                    writes=[b_tapchain], chan="tp")
                del tap_d[nm]

        tap("wm", wm_tab[:], [b_tabs])
        tap("wk", wk_tab[:], [b_tabs])
        tap("wb", wb_tab[:], [b_tabs])
        tap("ws", wspair[:], [b_tabs])

        n_out_dma = [0]

        for tau in range(min(NTILE, _NT)):
            b = tau % 2
            s_ = tau // 2
            xb_i = tau % 2
            first = (s_ == 0)
            tok0 = s_ * TT

            bx = b_xb[xb_i]

            if tau < 2:
                src = bass.AP(x_d.tensor, (b * SEQ + tok0) * C,
                              [[C, 128], [128 * C, 4], [1, C]])
                dma("sync", xb[xb_i][:], src, writes=bx, chan="x%d" % xb_i)

            if tau + 2 < NTILE:
                b2 = (tau + 2) % 2
                t2 = (tau + 2) // 2 * TT
                src2 = bass.AP(x_d.tensor, (b2 * SEQ + t2) * C,
                               [[C, 128], [128 * C, 4], [1, C]])
                pass

            sl_ar = load_weight("ar")
            sl_ak = load_weight("ak")

            if _PHASE >= 2 and first:
                S.add("vector", lambda e: e.memset(
                    bass.AP(y_fm, 0, [[8 * 513, 128], [513, 8], [1, 1]]), 0.0),
                    writes=[b_yfm])
            elif _PHASE >= 2:
                S.add("scalar", (lambda b: lambda e: e.copy(
                    bass.AP(y_fm, 0, [[8 * 513, 128], [513, 8], [1, 1]]),
                    ycar[b][:]))(b),
                    reads=[b_ycar[b]], writes=[b_yfm])

            for g in range(GROUPS if _PHASE >= 2 else 0):
                xg = xb[xb_i][:, 1024 * g:1024 * g + 1024]
                m1, ri, mb, b_st = ln_stats(xg, bx[g], 1.0 / 32.0, 1.0 / 1024.0, EPS)
                ytm = y_tm[g % 2]
                S.add("vector", (lambda xg, m1, ri, ytm: lambda e: e.tensor_scalar(
                    ytm[:], xg, m1, ri, ALU.subtract, ALU.mult))(xg, m1, ri, ytm),
                    reads=[bx[g], b_st], writes=[b_ytm[g % 2]])
                for c_ in range(8):
                    S.add("tensor", (lambda c_, ytm: lambda e: e.transpose(
                        ptr[:, 128 * c_:128 * c_ + 128],
                        ytm[:, 128 * c_:128 * c_ + 128], eyebf[:]))(c_, ytm),
                        reads=[b_ytm[g % 2], b_tabs], writes=[b_ptr])
                S.add("scalar", (lambda g: lambda e: e.copy(
                    bass.AP(y_fm, 513 * 0 + 1 + 128 * g,
                            [[8 * 513, 128], [513, 8], [1, 128]]),
                    ptr[:]))(g),
                    reads=[b_ptr], writes=[b_yfm])

            if tau == 0:
                tap("x", xb[xb_i][:], list(bx))
                tap("yfm", y_fm[:], [b_yfm])
            if _PHASE >= 2:
                S.add("scalar", (lambda b: lambda e: e.copy(
                    ycar[b][:],
                    bass.AP(y_fm, 512, [[8 * 513, 128], [513, 8], [1, 1]])))(b),
                    reads=[b_yfm], writes=[b_ycar[b]])

            for c_ in range(8 if _PHASE >= 2 else 0):
                yc = bass.AP(y_fm, 513 * c_ + 1, [[8 * 513, 128], [1, 512]])
                ycs = bass.AP(y_fm, 513 * c_, [[8 * 513, 128], [1, 512]])
                S.add("vector", (lambda yc, ycs: lambda e: e.tensor_sub(
                    d_sc[:], yc, ycs))(yc, ycs),
                    reads=[b_yfm], writes=[b_dsc])
                for n, mx in [("r", "mixr"), ("k", "mixk"), ("v", "mixv"), ("g", "mixg")]:
                    S.add("vector", (lambda n, mx, c_, ycs: lambda e:
                          e.scalar_tensor_tensor(
                              xs_fm[n][:, 512 * c_:512 * c_ + 512], d_sc[:],
                              mixv_sb[mx][:, c_:c_ + 1], ycs, ALU.mult, ALU.add))(
                              n, mx, c_, ycs),
                        reads=[b_dsc, b_yfm, b_tabs], writes=[b_xs[n]])

            def std_proj(wslot, xs_name, dst, b_dst, evac):
                for ko in range(8):
                    pb = pp[ko % 2]
                    bpb = b_pp[ko % 2]
                    for ki in range(8):
                        S.add("tensor", (lambda ko, ki, wslot, xs_name: lambda e:
                              e.matmul(pp[ko % 2][:, :],
                                       wbuf[wslot][:, 1024 * ki + 128 * ko:
                                                    1024 * ki + 128 * ko + 128],
                                       xs_fm[xs_name][:, 512 * ki:512 * ki + 512],
                                       start=(ki == 0), stop=(ki == 7)))(
                                  ko, ki, wslot, xs_name),
                            reads=[b_w[wslot], b_xs[xs_name]], writes=[bpb])
                    S.add("scalar", (lambda ko, pb: lambda e: evac(ko, pb[:, :]))(ko, pb),
                          reads=[bpb], writes=[b_dst])

            if _PHASE >= 3:
                std_proj(sl_ar, "r", r_fm, b_r,
                         lambda ko, p: nc.scalar.copy(r_fm[:, 512 * ko:512 * ko + 512], p))
            sl_ag = load_weight("ag")
            if _PHASE >= 3:
                std_proj(sl_ak, "k", k_fm, b_k,
                         lambda ko, p: nc.scalar.copy(k_fm[:, 512 * ko:512 * ko + 512], p))
            sl_av = load_weight("av")
            if _PHASE >= 3:
                std_proj(sl_ag, "g", g_fm, b_g,
                         lambda ko, p: nc.scalar.activation(
                             g_fm[:, 512 * ko:512 * ko + 512], p, AF.Silu))
            sl_ao = load_weight("ao")

            if tau == 0:
                tap("r", r_fm[:], [b_r])
                tap("k", k_fm[:], [b_k])
                tap("g", g_fm[:], [b_g])
                tap("xsr", xs_fm["r"][:], [b_xs["r"]])
                tap("xsk", xs_fm["k"][:], [b_xs["k"]])
                tap("xsv", xs_fm["v"][:], [b_xs["v"]])
                tap("xsg", xs_fm["g"][:], [b_xs["g"]])
            for g in range(GROUPS if _PHASE >= 4 else 0):
                pg = g % 2
                for ki in range(8):
                    for half in range(2):
                        S.add("tensor", (lambda ki, half, g, sl_av: lambda e: e.matmul(
                            pact[:, 512 * half:512 * half + 512],
                            xs_fm["v"][:, 512 * ki + 128 * g:512 * ki + 128 * g + 128],
                            wbuf[sl_av][:, 1024 * ki + 512 * half:
                                         1024 * ki + 512 * half + 512],
                            start=(ki == 0), stop=(ki == 7)))(ki, half, g, sl_av),
                            reads=[b_w[sl_av], b_xs["v"]], writes=[b_pact])
                S.add("scalar", (lambda pg: lambda e: e.copy(v_tm[pg][:], pact[:]))(pg),
                      reads=[b_pact], writes=[b_vtm[pg]])

                for c_ in range(8):
                    S.add("tensor", (lambda c_, g: lambda e: e.transpose(
                        ptr[:, 128 * c_:128 * c_ + 128],
                        k_fm[:, 512 * c_ + 128 * g:512 * c_ + 128 * g + 128],
                        eyebf[:]))(c_, g),
                        reads=[b_k, b_tabs], writes=[b_ptr])
                S.add("vector", (lambda pg: lambda e: e.tensor_mul(
                    kkw[pg][:], ptr[:], wk_tab[:]))(pg),
                    reads=[b_ptr, b_tabs], writes=[b_kkw[pg]])

                for c_ in range(8):
                    S.add("vector", (lambda c_, g, pg: lambda e: e.tensor_mul(
                        rrw[pg][:, 128 * c_:128 * c_ + 128],
                        r_fm[:, 512 * c_ + 128 * g:512 * c_ + 128 * g + 128],
                        wb_tab[:, 128 * c_:128 * c_ + 128]))(c_, g, pg),
                        reads=[b_r, b_tabs], writes=[b_rrw[pg]])

                if tau == 0 and g == 0:
                    tap("vtm", v_tm[pg][:], [b_vtm[pg]])
                    tap("kkw", kkw[pg][:], [b_kkw[pg]])
                    tap("rrw", rrw[pg][:], [b_rrw[pg]])
                def att_round(r_, g):
                    prev = [None]
                    for hh in range(8):
                        h = 8 * r_ + hh
                        c_ = h // 2
                        po = 64 * (h % 2)
                        col = 128 * ((h - 8 * r_) // 2)
                        dstp = patt if h % 2 == 0 else pp[1]
                        dstb = b_patt[0] if h % 2 == 0 else b_pp[1]
                        op = S.add("tensor", (lambda h, c_, po, col, dstp, g: lambda e:
                              e.matmul(dstp[:, col:col + 128],
                                       k_fm[po:po + 64, 512 * c_ + 128 * g:
                                            512 * c_ + 128 * g + 128],
                                       r_fm[po:po + 64, 512 * c_ + 128 * g:
                                            512 * c_ + 128 * g + 128],
                                       start=True, stop=True))(h, c_, po, col, dstp, g),
                            reads=[b_k, b_r], writes=[dstb])
                        if _ATTSER and prev[0] is not None:
                            op.fdeps.append(prev[0])
                        prev[0] = op

                def att_evac(r_):
                    if _P5 < 2:
                        return
                    wm_e = bass.AP(wm_tab, 1024 * r_, [[2048, 128], [256, 4], [1, 128]])
                    wm_o = bass.AP(wm_tab, 1024 * r_ + 128,
                                   [[2048, 128], [256, 4], [1, 128]])
                    S.add("vector", (lambda wm_e: lambda e: e.tensor_mul(
                        attw[0][:], patt[:], wm_e))(wm_e),
                        reads=[b_patt[0], b_tabs], writes=[b_attw[0]])
                    S.add("vector", (lambda wm_o: lambda e: e.tensor_mul(
                        attw[1][:], pp[1][:], wm_o))(wm_o),
                        reads=[b_pp[1], b_tabs], writes=[b_attw[1]])

                def outs_round(r_, g):
                    if _P5 < 3:
                        return
                    for hh in range(8):
                        h = 8 * r_ + hh
                        aw = h % 2
                        col = 128 * ((h - 8 * r_) // 2)
                        st1 = (h == 0 or h == 8)
                        S.add("tensor", (lambda h, aw, col, st1, pg: lambda e: e.matmul(
                            pout[:, 64 * h:64 * h + 64],
                            attw[aw][:, col:col + 128],
                            v_tm[pg][:, 64 * h:64 * h + 64],
                            start=st1, stop=(False if _P5 >= 4 else
                                             (h == 7 or h == 15))))(h, aw, col, st1, pg),
                            reads=[b_attw[aw], b_vtm[pg]], writes=[b_pout[h]])
                        if _P5 < 4:
                            continue
                        S.add("tensor", (lambda h, b, pg: lambda e: e.matmul(
                            pout[:, 64 * h:64 * h + 64],
                            rrw[pg][:, 128 * (h // 2):128 * (h // 2) + 128],
                            s_bf[b][:, 128 * (h // 2) + 64 * (h % 2):
                                     128 * (h // 2) + 64 * (h % 2) + 64],
                            start=False, stop=(h == 7 or h == 15)))(h, b, pg),
                            reads=[b_rrw[pg], b_sbf[b]], writes=[b_pout[h]])

                def st_all(g):
                    if _P5 < 5:
                        return
                    last_even = None
                    for h in [0, 2, 4, 6, 8, 10, 12, 14, 1, 3, 5, 7, 9, 11, 13, 15]:
                        op = S.add("tensor", (lambda h, pg: lambda e: e.matmul(
                            pp[0][64 * (h % 2):64 * (h % 2) + 64,
                                  64 * (h // 2):64 * (h // 2) + 64],
                            kkw[pg][:, 128 * (h // 2) + 64 * (h % 2):
                                    128 * (h // 2) + 64 * (h % 2) + 64],
                            v_tm[pg][:, 64 * h:64 * h + 64],
                            start=(h in (0, 1)), stop=(h in (14, 15)),
                            tile_position=(0, 64 * (h % 2))))(h, pg),
                            reads=[b_kkw[pg], b_vtm[pg]], writes=[b_pp[0]])
                        if h == 14:
                            last_even = op
                        if h == 1 and last_even is not None:
                            op.fdeps.append(last_even)

                if _P5 < 1:
                    continue
                att_round(0, g)
                att_evac(0)
                att_round(1, g)
                outs_round(0, g)
                att_evac(1)
                outs_round(1, g)
                st_all(g)

                if tau == 0 and g == 0 and _P5 >= 2:
                    tap("attwe", attw[0][:], [b_attw[0]])
                    tap("attwo", attw[1][:], [b_attw[1]])
                for h in range(16 if _P5 >= 6 else 0):
                    S.add("vector", (lambda h, b: lambda e: e.scalar_tensor_tensor(
                        s_f32[b][64 * (h % 2):64 * (h % 2) + 64,
                                 128 * (h // 2) + 64 * (h % 2):
                                 128 * (h // 2) + 64 * (h % 2) + 64],
                        s_f32[b][64 * (h % 2):64 * (h % 2) + 64,
                                 128 * (h // 2) + 64 * (h % 2):
                                 128 * (h // 2) + 64 * (h % 2) + 64],
                        wspair[64 * (h % 2):64 * (h % 2) + 64, h // 2:h // 2 + 1],
                        pp[0][64 * (h % 2):64 * (h % 2) + 64,
                              64 * (h // 2):64 * (h // 2) + 64],
                        ALU.mult, ALU.add))(h, b),
                        reads=[b_pp[0], b_sf[b]], writes=[b_sf[b]])
                if _P5 >= 6:
                    S.add("scalar", (lambda b: lambda e: e.copy(
                        s_bf[b][:], s_f32[b][:]))(b),
                        reads=[b_sf[b]], writes=[b_sbf[b]])

                if _PHASE < 6:
                    continue
                if tau == 0 and g == 0 and _P5 >= 6:
                    tap("sbf", s_bf[b][:], [b_sbf[b]])
                _DBG.append(("evac", tau, g,
                    S.add("scalar", (lambda pg: lambda e: e.copy(
                        xo_sb[pg][:], pout[:]))(pg),
                        reads=b_pout, writes=[b_xo[pg]])))
                gs = gstat[pg]
                b_gs = b_gstat[pg]
                sumx = gs[:, 0:16]
                m1g = gs[:, 16:32]
                negm = gs[:, 32:48]
                sgg = gs[:, 48:64]
                _DBG.append(("sumred", tau, g,
                    S.add("vector", (lambda pg, sumx: lambda e: e.tensor_reduce(
                        sumx, xo_sb[pg][:].rearrange("p (a b) -> p a b", a=16), AX,
                        ALU.add))(pg, sumx),
                        reads=[b_xo[pg]], writes=[b_gs], hz=True)))
                S.add("vector", (lambda sumx, m1g: lambda e: e.tensor_scalar(
                    m1g, sumx, 1.0 / 64.0, None, ALU.mult))(sumx, m1g),
                    reads=[b_gs], writes=[b_gs], small=True)
                S.add("vector", (lambda m1g, negm: lambda e: e.tensor_scalar(
                    negm, m1g, -1.0, None, ALU.mult))(m1g, negm),
                    reads=[b_gs], writes=[b_gs], small=True)
                for h in range(16):
                    S.add("scalar", (lambda h, pg, negm: lambda e: e.activation(
                        t_tm[pg][:, 64 * h:64 * h + 64],
                        xo_sb[pg][:, 64 * h:64 * h + 64], AF.Identity,
                        bias=negm[:, h:h + 1]))(h, pg, negm),
                        reads=[b_xo[pg], b_gs], writes=[b_ttm[pg]])
                if tau == 0 and g == 0:
                    tap("cen", t_tm[pg][:], [b_ttm[pg]])
                S.add("vector", (lambda pg: lambda e: e.tensor_mul(
                    sqs[pg][:], t_tm[pg][:], t_tm[pg][:]))(pg),
                    reads=[b_ttm[pg]], writes=[b_sqs[pg]])
                S.add("vector", (lambda pg, negm: lambda e: e.tensor_reduce(
                    negm, sqs[pg][:].rearrange("p (a b) -> p a b", a=16), AX,
                    ALU.add))(pg, negm),
                    reads=[b_sqs[pg]], writes=[b_gs], hz=True)
                S.add("scalar", (lambda negm, sgg: lambda e: e.activation(
                    sgg, negm, AF.Sqrt, scale=1.0 / 64.0, bias=GN_EPS))(negm, sgg),
                    reads=[b_gs, b_tabs], writes=[b_gs])
                vng = gs[:, 64:80]
                S.add("vector", (lambda sgg, vng: lambda e: e.reciprocal(vng, sgg))(
                    sgg, vng),
                    reads=[b_gs], writes=[b_gs], hz=True)
                for h in range(16):
                    S.add("scalar", (lambda h, pg, vng: lambda e: e.activation(
                        sqs[pg][:, 64 * h:64 * h + 64],
                        t_tm[pg][:, 64 * h:64 * h + 64], AF.Identity,
                        scale=vng[:, h:h + 1]))(h, pg, vng),
                        reads=[b_gs, b_ttm[pg]], writes=[b_sqs[pg]])
                if tau == 0 and g == 0:
                    tap("xo", xo_sb[pg][:], [b_xo[pg]])
                    tap("gs", gs[:], [b_gs])
                    tap("ttm", sqs[pg][:], [b_sqs[pg]])
                for c_ in range(8):
                    S.add("tensor", (lambda c_, pg: lambda e: e.transpose(
                        ptr[:, 128 * c_:128 * c_ + 128],
                        sqs[pg][:, 128 * c_:128 * c_ + 128], eyebf[:]))(c_, pg),
                        reads=[b_sqs[pg], b_tabs], writes=[b_ptr])
                for c_ in range(8):
                    S.add("scalar", (lambda c_, g: lambda e: e.activation(
                        yg_fm[:, 512 * c_ + 128 * g:512 * c_ + 128 * g + 128],
                        ptr[:, 128 * c_:128 * c_ + 128], AF.Identity,
                        scale=mixv_sb["lnw"][:, c_:c_ + 1],
                        bias=mixv_sb["lnb"][:, c_:c_ + 1]))(c_, g),
                        reads=[b_ptr, b_tabs], writes=[b_yg])
                for c_ in range(8):
                    S.add("vector", (lambda c_, g: lambda e: e.tensor_mul(
                        yg_fm[:, 512 * c_ + 128 * g:512 * c_ + 128 * g + 128],
                        yg_fm[:, 512 * c_ + 128 * g:512 * c_ + 128 * g + 128],
                        g_fm[:, 512 * c_ + 128 * g:512 * c_ + 128 * g + 128]))(c_, g),
                        reads=[b_g, b_yg], writes=[b_yg])

                for ki in range(8):
                    for half in range(2):
                        S.add("tensor", (lambda ki, half, g, sl_ao: lambda e: e.matmul(
                            pact[:, 512 * half:512 * half + 512],
                            yg_fm[:, 512 * ki + 128 * g:512 * ki + 128 * g + 128],
                            wbuf[sl_ao][:, 1024 * ki + 512 * half:
                                         1024 * ki + 512 * half + 512],
                            start=(ki == 0), stop=(ki == 7)))(ki, half, g, sl_ao),
                            reads=[b_w[sl_ao], b_yg], writes=[b_pact])

                m1, ri, mb, b_st = ln_stats(pact[:], b_pact, 1.0 / 32.0,
                                            1.0 / 1024.0, EPS)
                S.add("vector", (lambda pg, m1, ri: lambda e: e.tensor_scalar(
                    t_bf[pg][:], pact[:], m1, ri, ALU.subtract, ALU.mult))(pg, m1, ri),
                    reads=[b_pact, b_st], writes=[b_tbf[pg]])
                xg = xb[xb_i][:, 1024 * g:1024 * g + 1024]
                S.add("vector", (lambda xg, pg: lambda e: e.tensor_add(
                    xg, xg, t_bf[pg][:]))(xg, pg),
                    reads=[b_tbf[pg]] + [bx[g]], writes=[bx[g]])

            if tau == 0:
                tap("yg", yg_fm[:], [b_yg])
                tap("xmid", xb[xb_i][:], list(bx))
            sl_fk = load_weight("fk")
            sl_fr = load_weight("fr")

            if _PHASE >= 7 and first:
                S.add("vector", lambda e: e.memset(
                    bass.AP(y_fm, 0, [[8 * 513, 128], [513, 8], [1, 1]]), 0.0),
                    reads=[b_yfm], writes=[b_yfm])
            elif _PHASE >= 7:
                S.add("scalar", (lambda b: lambda e: e.copy(
                    bass.AP(y_fm, 0, [[8 * 513, 128], [513, 8], [1, 1]]),
                    y2car[b][:]))(b),
                    reads=[b_y2car[b], b_yfm], writes=[b_yfm])
            for g in range(GROUPS if _PHASE >= 7 else 0):
                xg = xb[xb_i][:, 1024 * g:1024 * g + 1024]
                m1, ri, mb, b_st = ln_stats(xg, bx[g], 1.0 / 32.0, 1.0 / 1024.0, EPS)
                ytm = y_tm[g % 2]
                S.add("vector", (lambda xg, m1, ri, ytm: lambda e: e.tensor_scalar(
                    ytm[:], xg, m1, ri, ALU.subtract, ALU.mult))(xg, m1, ri, ytm),
                    reads=[bx[g], b_st], writes=[b_ytm[g % 2]])
                for c_ in range(8):
                    S.add("tensor", (lambda c_, ytm: lambda e: e.transpose(
                        ptr[:, 128 * c_:128 * c_ + 128],
                        ytm[:, 128 * c_:128 * c_ + 128], eyebf[:]))(c_, ytm),
                        reads=[b_ytm[g % 2], b_tabs], writes=[b_ptr])
                S.add("scalar", (lambda g: lambda e: e.copy(
                    bass.AP(y_fm, 1 + 128 * g, [[8 * 513, 128], [513, 8], [1, 128]]),
                    ptr[:]))(g),
                    reads=[b_ptr], writes=[b_yfm])
            if _PHASE >= 7:
                S.add("scalar", (lambda b: lambda e: e.copy(
                    y2car[b][:],
                    bass.AP(y_fm, 512, [[8 * 513, 128], [513, 8], [1, 1]])))(b),
                    reads=[b_yfm], writes=[b_y2car[b]])

            for c_ in range(8 if _PHASE >= 7 else 0):
                yc = bass.AP(y_fm, 513 * c_ + 1, [[8 * 513, 128], [1, 512]])
                ycs = bass.AP(y_fm, 513 * c_, [[8 * 513, 128], [1, 512]])
                S.add("vector", (lambda yc, ycs: lambda e: e.tensor_sub(
                    d_sc[:], yc, ycs))(yc, ycs),
                    reads=[b_yfm], writes=[b_dsc])
                for n, mx in [("r", "mur"), ("k", "muk")]:
                    S.add("vector", (lambda n, mx, c_, ycs: lambda e:
                          e.scalar_tensor_tensor(
                              xs_fm[n][:, 512 * c_:512 * c_ + 512], d_sc[:],
                              mixv_sb[mx][:, c_:c_ + 1], ycs, ALU.mult, ALU.add))(
                              n, mx, c_, ycs),
                        reads=[b_dsc, b_yfm, b_tabs], writes=[b_xs[n]])

            if _PHASE >= 7:
                std_proj(sl_fk, "k", r_fm, b_r,
                         lambda ko, p: nc.scalar.activation(
                             r_fm[:, 512 * ko:512 * ko + 512], p, AF.Relu))
            sl_fv = load_weight("fv")
            for ko in range(8 if _PHASE >= 7 else 0):
                S.add("scalar", (lambda ko: lambda e: e.activation(
                    k_fm[:, 512 * ko:512 * ko + 512],
                    r_fm[:, 512 * ko:512 * ko + 512], AF.Square))(ko),
                    reads=[b_r], writes=[b_k])

            if tau == 0:
                tap("y2fm", y_fm[:], [b_yfm])
                tap("kact", k_fm[:], [b_k])
            for g in range(GROUPS):
                pg = g % 2
                for ki in range(8 if _PHASE >= 8 else 0):
                    for half in range(2):
                        S.add("tensor", (lambda ki, half, g, sl_fr: lambda e: e.matmul(
                            pact[:, 512 * half:512 * half + 512],
                            xs_fm["r"][:, 512 * ki + 128 * g:512 * ki + 128 * g + 128],
                            wbuf[sl_fr][:, 1024 * ki + 512 * half:
                                         1024 * ki + 512 * half + 512],
                            start=(ki == 0), stop=(ki == 7)))(ki, half, g, sl_fr),
                            reads=[b_w[sl_fr], b_xs["r"]], writes=[b_pact])
                if _PHASE >= 8:
                    S.add("scalar", (lambda pg: lambda e: e.activation(
                        r2_tm[pg][:], pact[:], AF.Sigmoid))(pg),
                        reads=[b_pact], writes=[b_r2[pg]])
                for ki in range(8 if _PHASE >= 8 else 0):
                    for half in range(2):
                        S.add("tensor", (lambda ki, half, g, sl_fv: lambda e: e.matmul(
                            pact[:, 512 * half:512 * half + 512],
                            k_fm[:, 512 * ki + 128 * g:512 * ki + 128 * g + 128],
                            wbuf[sl_fv][:, 1024 * ki + 512 * half:
                                         1024 * ki + 512 * half + 512],
                            start=(ki == 0), stop=(ki == 7)))(ki, half, g, sl_fv),
                            reads=[b_w[sl_fv], b_k], writes=[b_pact])
                if _PHASE >= 8:
                    if tau == 0 and g == 0:
                        tap("r2", r2_tm[pg][:], [b_r2[pg]])
                    S.add("vector", (lambda pg: lambda e: e.tensor_mul(
                        fo_bf[pg][:], r2_tm[pg][:], pact[:]))(pg),
                        reads=[b_pact, b_r2[pg]], writes=[b_fo[pg]])
                    m1, ri, mb, b_st = ln_stats(fo_bf[pg][:], b_fo[pg], 1.0 / 32.0,
                                                1.0 / 1024.0, EPS)
                    if tau == 0 and g == 0:
                        tap("fo", fo_bf[pg][:], [b_fo[pg]])
                    S.add("vector", (lambda pg, m1, ri: lambda e: e.tensor_scalar(
                        t_bf[pg][:], fo_bf[pg][:], m1, ri, ALU.subtract, ALU.mult))(
                        pg, m1, ri),
                        reads=[b_fo[pg], b_st], writes=[b_tbf[pg]])
                    xg = xb[xb_i][:, 1024 * g:1024 * g + 1024]
                    S.add("vector", (lambda xg, pg: lambda e: e.tensor_add(
                        xg, xg, t_bf[pg][:]))(xg, pg),
                        reads=[b_tbf[pg]], writes=[bx[g]])
                xg = xb[xb_i][:, 1024 * g:1024 * g + 1024]
                dst = bass.AP(out_d.tensor, (b * SEQ + tok0 + 128 * g) * C,
                              [[C, 128], [1, C]])
                dma("gpsimd", dst, xg, reads=[bx[g]], chan="o%d" % xb_i)
                n_out_dma[0] += 1

            if tau + 2 < NTILE:
                b2 = (tau + 2) % 2
                t2 = (tau + 2) // 2 * TT
                src2 = bass.AP(x_d.tensor, (b2 * SEQ + t2) * C,
                               [[C, 128], [128 * C, 4], [1, C]])
                dma("sync", xb[xb_i][:], src2, writes=bx, chan="x%d" % xb_i)

        gp_chans = sorted({op.chan for op in S.ops["gpsimd"] if op.kind == "d"})
        gp_tot = {c: sum(1 for op in S.ops["gpsimd"] if op.kind == "d" and op.chan == c)
                  for c in gp_chans}

        def final_wait(e):
            inst = None
            for c in gp_chans:
                inst = e.wait_ge(sems[("c", c)], 16 * gp_tot[c])
            return None
        S.add("gpsimd", final_wait, kind="x")

        S.finalize()
        all_chans = sorted({op.chan for e in S.ENGINES for op in S.ops[e]
                            if op.kind == "d"})
        with ExitStack() as sctx:
            for e in ("gpsimd", "vector", "scalar", "tensor", "sync"):
                sems[(e, "x")] = sctx.enter_context(nc.semaphore("sem_" + e))
            for c in all_chans:
                sems[("c", c)] = sctx.enter_context(nc.semaphore("sem_c_" + c))
            with nc.Block() as block:
                S.emit(block, nc, sems)

    return nc


_NC_CACHE = [None]


def kernel(L0_attn_xq, L0_attn_time_mix_k, L0_attn_time_mix_v, L0_attn_time_mix_r,
           L0_attn_time_mix_g, L0_attn_time_decay, L0_attn_time_faaaa,
           L0_attn_W_r, L0_attn_W_k, L0_attn_W_v, L0_attn_W_g, L0_attn_W_o,
           L0_attn_ln_w, L0_attn_ln_b, L0_ffn_W_r, L0_ffn_W_k, L0_ffn_W_v,
           L0_ffn_mu_r, L0_ffn_mu_k):
    x = np.ascontiguousarray(np.asarray(L0_attn_xq, dtype=np.float32))
    ws = {
        "ar": L0_attn_W_r, "ak": L0_attn_W_k, "av": L0_attn_W_v,
        "ag": L0_attn_W_g, "ao": L0_attn_W_o,
        "fr": L0_ffn_W_r, "fk": L0_ffn_W_k, "fv": L0_ffn_W_v,
    }
    wT = {n: np.ascontiguousarray(np.asarray(w, dtype=np.float32).T)
          for n, w in ws.items()}
    vecs = {
        "mixr": L0_attn_time_mix_r, "mixk": L0_attn_time_mix_k,
        "mixv": L0_attn_time_mix_v, "mixg": L0_attn_time_mix_g,
        "mur": L0_ffn_mu_r, "muk": L0_ffn_mu_k,
        "lnw": L0_attn_ln_w, "lnb": L0_attn_ln_b,
    }
    vecs = {n: np.ascontiguousarray(np.asarray(v, dtype=np.float32).reshape(1024))
            for n, v in vecs.items()}
    td = np.ascontiguousarray(np.asarray(L0_attn_time_decay, dtype=np.float32))
    u = np.ascontiguousarray(np.asarray(L0_attn_time_faaaa, dtype=np.float32))

    i128 = np.arange(128)
    consts = {
        "c_eyebf": np.eye(128, dtype=ml_dtypes.bfloat16),
        "c_ind16bf": (np.arange(1024)[None, :] // 64 ==
                      np.arange(16)[:, None]).astype(ml_dtypes.bfloat16),

        "c_diffm1": (i128[None, :] - i128[:, None] - 1).astype(np.float32),
        "c_tri": (i128[None, :] > i128[:, None]).astype(np.float32),
        "c_eyef": np.eye(128, dtype=np.float32),
        "c_iota16": np.tile(np.arange(128, dtype=np.float32), (16, 1)),
        "c_riota16": np.tile(127.0 - np.arange(128, dtype=np.float32), (16, 1)),
        "c_f32s": np.array(CONST_F32S, dtype=np.float32),
    }
    pos = np.arange(1024)
    consts["c_indpcbf"] = (np.arange(16)[:, None] ==
                           (2 * (pos // 128) + (pos % 128) // 64)[None, :]
                           ).astype(ml_dtypes.bfloat16)

    if _NC_CACHE[0] is None:
        _NC_CACHE[0] = _build_program()
    nc = _NC_CACHE[0]

    in_maps = []
    for i in range(8):
        m = {"x": np.ascontiguousarray(x[2 * i:2 * i + 2])}
        for n in W_NAMES:
            m["wT_" + n] = wT[n]
        for n, v in vecs.items():
            m["v_" + n] = v
        m["td"] = td
        m["u"] = u
        m.update(consts)
        in_maps.append(m)

    res = run_bass_kernel_spmd(nc, in_maps, core_ids=list(range(8)))
    out = np.concatenate([np.asarray(r["out"]) for r in res.results], axis=0)
    return out.astype(np.float32, copy=False)
